# version 02 (final)
# baseline (speedup 1.0000x reference)
import numpy as np
from contextlib import ExitStack

import concourse.bass as bass
import concourse.mybir as mybir
from concourse.bass_utils import run_bass_kernel_spmd

F32 = mybir.dt.float32
BF16 = mybir.dt.bfloat16
AF = mybir.ActivationFunctionType
ALU = mybir.AluOpType

C = 1024
T = 2048
H = 16
DH = 64
NKT = 8
SC = 512
NSC = T // SC
CH = 128
NCHS = SC // CH
EPS_LN = 1e-5
EPS_GLA = 1e-6
PI = float(np.pi)
DEBUG_STAGE = 0

ENGS = ("tensor", "vector", "scalar", "sync", "gpsimd")
NUMOFF = [(h // 7) * 512 + (h % 7) * 65 for h in range(16)]
UOFF = [(j // 7) * 512 + (j % 7) * 65 for j in range(8)]


class Prog:
    def __init__(self):
        self.semv = {}
        self.order = []
        self.reset_phase()

    def reset_phase(self):
        self.ops = {e: [] for e in ENGS}
        self.last = {e: None for e in ENGS}
        self.dirty = {e: False for e in ENGS}
        if not hasattr(self, "chain_tok"):
            self.chain_tok = {e: None for e in ENGS}

    def sem(self, name):
        if name not in self.semv:
            self.semv[name] = 0
            self.order.append(name)
        return name

    def wait(self, eng, tok):
        if tok is not None:
            self.ops[eng].append(["w", tok[0], tok[1]])

    AUTOCHAIN = ("vector", "scalar", "gpsimd")

    def op(self, eng, fn, waits=(), inc=None, inc_amt=1):
        is_dma = inc_amt == 16
        wl = [t for t in waits if t is not None]
        if eng in self.AUTOCHAIN and not is_dma:
            ct = self.chain_tok.get(eng)
            if ct is not None and ct not in wl:
                wl.insert(0, ct)
        for t in wl:
            self.wait(eng, t)
        tok = None
        if inc is not None:
            inc = self.sem(inc)
            self.semv[inc] += inc_amt
            tok = (inc, self.semv[inc])
        self.ops[eng].append(["o", fn, inc, inc_amt])
        self.last[eng] = len(self.ops[eng]) - 1
        self.dirty[eng] = True
        if eng in self.AUTOCHAIN and not is_dma and tok is not None:
            self.chain_tok[eng] = tok
        return tok

    def force_tokens(self):
        toks = []
        for e in ENGS:
            if not self.dirty[e]:
                continue
            entry = self.ops[e][self.last[e]]
            if entry[0] != "o":
                continue
            if entry[2] is None:
                s = self.sem("E_" + e)
                self.semv[s] += 1
                entry[2] = s
                entry[3] = 1
            toks.append((entry[2], self.semv[entry[2]]))
            self.dirty[e] = False
        return toks

    def barrier(self):
        toks = self.force_tokens()
        seen = {t[0] for t in toks}
        for s, v in self.semv.items():
            if v > 0 and s not in seen:
                toks.append((s, v))
        for e in ENGS:
            for t in toks:
                self.ops[e].append(["w", t[0], t[1]])


def build(debug_stage=DEBUG_STAGE):
    nc = bass.Bass("TRN2", target_bir_lowering=False)
    prog = Prog()

    xin = nc.dram_tensor("xin", [128, NKT * T], F32, kind="ExternalInput")
    out = nc.dram_tensor("out", [128, NKT * T], F32, kind="ExternalOutput")
    xs1 = nc.dram_tensor("xs1", [128, NKT * T], F32)
    xs2 = nc.dram_tensor("xs2", [128, NKT * T], F32)
    xs3 = nc.dram_tensor("xs3", [128, NKT * T], F32)

    wdr = {}
    for n in ["aWq", "aWk", "aWv", "aWo", "bWq", "bWk", "bWv", "bWo",
              "fWr", "fWk", "fWv"]:
        wdr[n] = nc.dram_tensor(n, [C, C], F32, kind="ExternalInput")
    wdr["aWa"] = nc.dram_tensor("aWa", [C, H], F32, kind="ExternalInput")
    for n in ["b_u", "b_v", "b_bo", "f_mur", "f_muk"]:
        wdr[n] = nc.dram_tensor(n, [128, 8], F32, kind="ExternalInput")
    wdr["a_ba"] = nc.dram_tensor("a_ba", [H, 1], F32, kind="ExternalInput")

    es = ExitStack()

    def sb(name, shape, dt=BF16):
        return es.enter_context(nc.sbuf_tensor(name, shape, dt))

    with es:
        W0 = sb("W0", [128, NKT * C]); W1 = sb("W1", [128, NKT * C])
        W2 = sb("W2", [128, NKT * C]); W3 = sb("W3", [128, NKT * C])
        aWa_sb = sb("aWa_sb", [128, NKT * H])
        vec_sb = sb("vec_sb", [128, 40], F32)
        ba2 = sb("ba2", [H, 1], F32)
        onesLN = sb("onesLN", [128, 128])
        ones1_64f = sb("ones1_64f", [1, 64], F32)
        ident_f = sb("ident_f", [128, 128], F32)
        mask_st = sb("mask_st", [128, 128])
        m2diag = sb("m2diag", [128, 128])
        m2up1 = sb("m2up1", [128, 128])
        mcmask = sb("mcmask", [128, 128])
        t1dup = sb("t1dup", [128, T])
        t2dup = sb("t2dup", [128, T])
        tswap = sb("tswap", [128, T])
        fgm = sb("fgm", [128, 64])
        ones128b = sb("ones128b", [128, 1])

        xbf = sb("xbf", [128, NKT * SC])
        x2b = sb("x2b", [128, NKT * SC])
        mubf = sb("mubf", [128, SC])
        scrA = sb("scrA", [128, SC], F32)
        scrB = sb("scrB", [128, SC], F32)
        absc = sb("absc", [128, SC], F32)
        tscr = sb("tscr", [128, SC])
        xres = sb("xres", [128, NKT * SC], F32)
        escr1 = sb("escr1", [128, C], F32)
        escr2 = sb("escr2", [128, C], F32)
        cvals = sb("cvals", [128, 4], F32)
        setup_scope = ExitStack()
        trig_s1 = setup_scope.enter_context(nc.sbuf_tensor("trig_s1", [32, T], F32))
        trig_s2 = setup_scope.enter_context(nc.sbuf_tensor("trig_s2", [32, T], F32))
        omega = setup_scope.enter_context(nc.sbuf_tensor("omega", [32, 1], F32))
        trig_s3 = setup_scope.enter_context(nc.sbuf_tensor("trig_s3", [32, T], F32))
        trig_i = setup_scope.enter_context(nc.sbuf_tensor("trig_i", [32, T], mybir.dt.int32))

        for s in ["T", "V", "S", "G",
                  "E_tensor", "E_vector", "E_scalar", "E_sync", "E_gpsimd"]:
            prog.sem(s)
        semhs = {}

        def do_flush():
            for sname in prog.order:
                if sname not in semhs:
                    semhs[sname] = es.enter_context(nc.semaphore(sname))
            ops = prog.ops
            if any(ops[e] for e in ENGS):
                with nc.Block() as block:
                    for e in ENGS:
                        if not ops[e]:
                            continue

                        def body(h, lst=tuple(ops[e])):
                            for item in lst:
                                if item[0] == "w":
                                    h.wait_ge(semhs[item[1]], item[2])
                                else:
                                    ins = item[1](h)
                                    if item[2] is not None:
                                        ins.then_inc(semhs[item[2]], item[3])

                        getattr(block, e)(body)
            prog.reset_phase()

        def t_op(fn, waits=(), inc="T"):
            return prog.op("tensor", fn, waits, inc)

        def v_op(fn, waits=(), inc="V"):
            return prog.op("vector", fn, waits, inc)

        def s_op(fn, waits=(), inc="S"):
            return prog.op("scalar", fn, waits, inc)

        def g_op(fn, waits=(), inc="G"):
            return prog.op("gpsimd", fn, waits, inc)

        dma_rr = {"DSY": 0, "DWG": 0, "DST": 0}

        def _dma(eng, base, fn, waits):
            i = dma_rr[base] % 4
            dma_rr[base] += 1
            sname = f"{base}{i}"
            prog.sem(sname)
            w = list(waits)
            if prog.semv[sname] > 0:
                w.append((sname, prog.semv[sname]))
            return prog.op(eng, fn, w, sname, 16)

        def dma_sync(fn, waits=()):
            return _dma("sync", "DSY", fn, waits)

        def dma_cast(fn, waits=()):
            return _dma("gpsimd", "DWG", fn, waits)

        def dma_store(fn, waits=()):
            return _dma("sync", "DST", fn, waits)

        def dram_slice(dr, sc):
            return bass.AP(dr, sc * SC, [[NKT * T, 128], [T, NKT], [1, SC]])

        def wload(dst, name, waits=()):
            src = wdr[name]
            cols = src.shape[1]
            return dma_cast(lambda h, dst=dst, src=src, cols=cols: h.dma_start(
                dst[:, 0:NKT * cols],
                bass.AP(src, 0, [[cols, 128], [128 * cols, NKT], [1, cols]])),
                waits=waits)

        wload(W0, "aWq"); wload(W1, "aWk"); wload(W2, "aWv"); wload(W3, "aWo")
        wload(aWa_sb, "aWa")
        for i, n in enumerate(["b_u", "b_v", "b_bo", "f_mur", "f_muk"]):
            dma_sync(lambda h, i=i, n=n: h.dma_start(
                vec_sb[:, 8 * i:8 * i + 8], wdr[n][:]))
        tok_ba = dma_sync(lambda h: h.dma_start(ba2[:], wdr["a_ba"][:]))

        g_op(lambda h: h.memset(onesLN[:], 1.0 / 1024.0))
        v_op(lambda h: h.memset(cvals[:, 0:1], 0.0))
        v_op(lambda h: h.memset(cvals[:, 1:2], EPS_LN))
        v_op(lambda h: h.memset(cvals[:, 2:3], PI))
        v_op(lambda h: h.memset(cvals[:, 3:4], -16.0))
        SMXB_AP = cvals[:, 3:4]
        v_op(lambda h: h.memset(ones1_64f[:], 1.0))
        _tkm0 = g_op(lambda h: h.memset(ident_f[:], 0.0))
        g_op(lambda h: h.affine_select(ident_f[:], ident_f[:], pattern=[[-1, 128]],
                                       compare_op=ALU.not_equal, fill=1.0,
                                       base=0, channel_multiplier=1), waits=(_tkm0,))
        _tkm1 = g_op(lambda h: h.memset(mask_st[:], 1.0))
        g_op(lambda h: h.affine_select(mask_st[:], mask_st[:], pattern=[[1, 128]],
                                       compare_op=ALU.is_ge, fill=0.0,
                                       base=0, channel_multiplier=-1), waits=(_tkm1,))
        _tkm2 = g_op(lambda h: h.memset(m2diag[:], 1.0))
        g_op(lambda h: h.affine_select(m2diag[:], m2diag[:], pattern=[[-1, 128]],
                                       compare_op=ALU.is_ge, fill=0.0,
                                       base=-2, channel_multiplier=1), waits=(_tkm2,))
        tmc = g_op(lambda h: h.memset(mcmask[:], 0.0))
        g_op(lambda h: h.memset(mcmask[0:1, 127:128], 1.0), waits=(tmc,))
        _tkm3 = g_op(lambda h: h.memset(m2up1[:], 1.0))
        g_op(lambda h: h.affine_select(m2up1[:], m2up1[:], pattern=[[-1, 128]],
                                       compare_op=ALU.is_ge, fill=0.0,
                                       base=-1, channel_multiplier=1), waits=(_tkm3,))
        v_op(lambda h: h.tensor_scalar_add(ba2[:], ba2[:], 2.0), waits=(tok_ba,))
        g_op(lambda h: h.memset(ones128b[:], 1.0))
        tfg = g_op(lambda h: h.memset(fgm[:], 0.0))
        tfg = g_op(lambda h: h.affine_select(
            fgm[0:32, 0:32], fgm[0:32, 0:32], pattern=[[-1, 32]],
            compare_op=ALU.not_equal, fill=1.0, base=0, channel_multiplier=1),
            waits=(tfg,))
        tfg = g_op(lambda h: h.affine_select(
            fgm[32:64, 0:32], fgm[32:64, 0:32], pattern=[[-1, 32]],
            compare_op=ALU.not_equal, fill=1.0, base=0, channel_multiplier=1),
            waits=(tfg,))
        tfg = g_op(lambda h: h.affine_select(
            fgm[0:32, 32:64], fgm[0:32, 32:64], pattern=[[-1, 32]],
            compare_op=ALU.not_equal, fill=-1.0, base=0, channel_multiplier=1),
            waits=(tfg,))
        tfg = g_op(lambda h: h.affine_select(
            fgm[32:64, 32:64], fgm[32:64, 32:64], pattern=[[-1, 32]],
            compare_op=ALU.not_equal, fill=1.0, base=0, channel_multiplier=1),
            waits=(tfg,))
        for r0, c0, fv in ((64, 0, 1.0), (96, 0, 1.0), (64, 32, -1.0),
                           (96, 32, 1.0)):
            tfg = g_op(lambda h, r0=r0, c0=c0, fv=fv: h.affine_select(
                fgm[r0:r0 + 32, c0:c0 + 32], fgm[r0:r0 + 32, c0:c0 + 32],
                pattern=[[-1, 32]], compare_op=ALU.not_equal, fill=fv,
                base=0, channel_multiplier=1), waits=(tfg,))

        tok_ip = g_op(lambda h: h.iota(trig_s1[:, 0:1], pattern=[[1, 1]], base=0,
                                       channel_multiplier=1,
                                       allow_small_or_imprecise_dtypes=True))
        tok_om = s_op(lambda h: h.activation(omega[:], trig_s1[:, 0:1], AF.Exp,
                                             bias=0.0,
                                             scale=-float(np.log(10000.0) / 32.0)),
                      waits=(tok_ip,))
        tok_ij = g_op(lambda h: h.iota(trig_s2[:], pattern=[[1, T]], base=0,
                                       channel_multiplier=0,
                                       allow_small_or_imprecise_dtypes=True))

        TWO_PI = 2 * PI
        OFF2 = float(330 * TWO_PI)
        tok_sin = []
        for (base_off, aoff, dst) in ((0, 0.0, t1dup), (-T, OFF2, t2dup)):
            for (shift, rows) in ((0.0, (32, 96)), (PI / 2, (0, 64))):
                tk1 = v_op(lambda h, b=base_off: h.tensor_scalar(
                    trig_s1[:], trig_s2[:], float(b), omega[:, 0:1],
                    op0=ALU.add, op1=ALU.mult),
                    waits=tuple([tok_om, tok_ij] + tok_sin))
                tok_sin = []
                if aoff + shift != 0.0:
                    tk1 = v_op(lambda h, c=aoff + shift: h.tensor_scalar_add(
                        trig_s1[:], trig_s1[:], float(c)), waits=(tk1,))
                tk2 = v_op(lambda h: h.tensor_scalar_mul(
                    trig_s3[:], trig_s1[:], 1.0 / TWO_PI), waits=(tk1,))
                tk3 = v_op(lambda h: h.tensor_copy(trig_i[:], trig_s3[:]),
                           waits=(tk2,))
                tk4 = v_op(lambda h: h.tensor_copy(trig_s3[:], trig_i[:]),
                           waits=(tk3,))
                tk5 = v_op(lambda h: h.scalar_tensor_tensor(
                    trig_s1[:], trig_s3[:], -TWO_PI, trig_s1[:],
                    ALU.mult, ALU.add), waits=(tk4,))
                tk6 = v_op(lambda h: h.tensor_scalar(
                    trig_s1[:], trig_s1[:], -1.0, PI,
                    op0=ALU.mult, op1=ALU.add), waits=(tk5,))
                tk7 = v_op(lambda h: h.tensor_scalar(
                    trig_s1[:], trig_s1[:], PI, -PI,
                    op0=ALU.min, op1=ALU.max), waits=(tk6,))
                for r0 in rows:
                    tok_sin.append(s_op(lambda h, r0=r0, dst=dst: h.activation(
                        dst[r0:r0 + 32, :], trig_s1[:], AF.Sin,
                        bias=0.0, scale=1.0), waits=(tk7,)))
                if dst is t1dup:
                    for r0 in ((0, 64) if shift == 0.0 else (32, 96)):
                        tok_sin.append(s_op(lambda h, r0=r0: h.activation(
                            tswap[r0:r0 + 32, :], trig_s1[:], AF.Sin,
                            bias=0.0, scale=1.0), waits=(tk7,)))
        prog.barrier()
        do_flush()
        setup_scope.close()

        if debug_stage == 9:
            for scq in range(NSC):
                dma_store(lambda h, scq=scq: h.dma_start(
                    dram_slice(out, scq), dram_slice(xin, scq)))
            prog.barrier()
            do_flush()
            return nc

        def ln_slice(sc, src_dram, xhat_dst, xhat_off, xhat_stride, slot,
                     ps_mu, ps_e2, halo=0, load_res=True):
            xo = 0
            tok_xc = dma_cast(lambda h, sc=sc, src_dram=src_dram: h.dma_start(
                xbf[:, xo:xo + NKT * SC], dram_slice(src_dram, sc)))
            tok_xr = None
            if load_res:
                tok_xr = dma_sync(lambda h, sc=sc, src_dram=src_dram: h.dma_start(
                    xres[:, xo:xo + NKT * SC], dram_slice(src_dram, sc)))
            last_mu = None
            for kt in range(NKT):
                last_mu = t_op(lambda h, kt=kt: h.matmul(
                    ps_mu[:, 0:SC], onesLN[:],
                    xbf[:, xo + kt * SC: xo + (kt + 1) * SC],
                    start=(kt == 0), stop=(kt == NKT - 1)),
                    waits=(tok_xc,) if kt == 0 else ())
            tok_x2 = v_op(lambda h: h.tensor_mul(
                x2b[:], xbf[:, xo:xo + NKT * SC], xbf[:, xo:xo + NKT * SC]),
                waits=(tok_xc,))
            last_e2 = None
            for kt in range(NKT):
                last_e2 = t_op(lambda h, kt=kt: h.matmul(
                    ps_e2[:, 0:SC], onesLN[:], x2b[:, kt * SC:(kt + 1) * SC],
                    start=(kt == 0), stop=(kt == NKT - 1)),
                    waits=(tok_x2,) if kt == 0 else ())
            mo = 0
            v_op(lambda h: h.tensor_copy(mubf[:, mo:mo + SC], ps_mu[:, 0:SC]),
                 waits=(last_mu,))
            v_op(lambda h: h.tensor_mul(scrA[:], mubf[:, mo:mo + SC],
                                        mubf[:, mo:mo + SC]))
            tok_var = v_op(lambda h: h.tensor_sub(scrB[:], ps_e2[:, 0:SC],
                                                  scrA[:]), waits=(last_e2,))
            tok_ln = s_op(lambda h: h.activation(scrA[:], scrB[:], AF.Ln,
                                                 bias=cvals[:, 1:2], scale=1.0),
                          waits=(tok_var,))
            tok_a = s_op(lambda h: h.activation(absc[:, mo:mo + SC], scrA[:],
                                                AF.Exp, bias=0.0, scale=-0.5),
                         waits=(tok_ln,))
            tok_h = None
            for kt in range(NKT):
                off = xhat_off + kt * xhat_stride + halo
                v_op(lambda h, kt=kt: h.tensor_sub(
                    tscr[:], xbf[:, xo + kt * SC: xo + (kt + 1) * SC],
                    mubf[:, mo:mo + SC]), waits=(tok_a,) if kt == 0 else ())
                tok_h = v_op(lambda h, off=off: h.tensor_mul(
                    xhat_dst[:, off:off + SC], tscr[:], absc[:, mo:mo + SC]))
            return tok_h, tok_xr

        with ExitStack() as ph:
            def psb(name, shape, dt=BF16):
                return ph.enter_context(nc.sbuf_tensor(name, shape, dt))

            def pps(name, shape, dt=F32):
                return ph.enter_context(nc.psum_tensor(name, shape, dt))

            xhat = psb("a_xhat", [128, NKT * SC])
            qfm = psb("a_qfm", [128, NKT * SC])
            kfm = psb("a_kfm", [128, NKT * SC])
            ktok = psb("a_ktok", [128, NCHS * C])
            vaug = psb("a_vaug", [128, NCHS * H * 65])
            vtil = psb("a_vtil", [128, H * 65])
            pmb = psb("a_pm", [128, H * CH])
            a_sc = psb("a_asc", [H, SC], F32)
            acp = psb("a_acp", [H, 3 * CH], F32)
            atok = psb("a_atok", [128, 2 * H], F32)
            acr = psb("a_acr", [1, H], F32)
            acb2 = psb("a_acb2", [128, H], F32)
            saug = psb("a_saug", [128, 640])
            sscr = psb("a_sscr", [128, 640], F32)
            dsml = psb("a_dsml", [128, 2 * H], F32)
            otok = psb("a_otok", [128, NCHS * C], F32)
            ofm = psb("a_ofm", [128, NKT * SC])

            PA = pps("a_PA", [128, 512])
            PB = pps("a_PB", [128, 512])
            PNUM = pps("a_PNUM", [128, 1536])
            PU = pps("a_PU", [128, 640])
            PSC = pps("a_PSC", [128, 512])

            v_op(lambda h: h.memset(saug[:], 0.0))
            v_op(lambda h: h.memset(PU[:], 0.0))
            v_op(lambda h: h.memset(PNUM[:], 0.0))
            va4 = vaug[:].rearrange("p (c h e) -> p c h e", c=NCHS, h=H, e=65)
            g_op(lambda h: h.memset(va4[:, :, :, 64:65], 1.0))

            tok_snew = None
            tok_outevac = None
            tok_pm_used = {0: None, 1: None}
            pm_slot_tok = {}
            elu_tok = [None, None]

            for sc in range(NSC):
                xo = 0
                slot = 0
                tok_xh, tok_xr = ln_slice(sc, xin, xhat, 0, SC, slot, PA, PB)
                prog.barrier()
                if debug_stage == 8 and sc == 0:
                    dma_store(lambda h: h.dma_start(
                        dram_slice(out, 0), xres[:, 0:NKT * SC]))
                    prog.barrier()
                    do_flush()
                    return nc

                def xh(kt, t0=0, n=SC):
                    return xhat[:, kt * SC + t0: kt * SC + t0 + n]

                def elu_evac(ps_ap, dst_ap, n, wait_tok):
                    t1 = s_op(lambda h: h.activation(escr1[:, 0:n], ps_ap, AF.Exp,
                                                     bias=0.0, scale=1.0),
                              waits=(wait_tok, elu_tok[0]))
                    t2 = g_op(lambda h: h.tensor_scalar_min(
                        escr2[:, 0:n], escr1[:, 0:n], 1.0),
                        waits=(t1, elu_tok[1]))
                    elu_tok[0] = t2
                    t3 = v_op(lambda h: h.scalar_tensor_tensor(
                        dst_ap, ps_ap, 0.0, escr2[:, 0:n], ALU.max, ALU.add),
                        waits=(t2,))
                    elu_tok[1] = t3
                    return t3

                btok = {}
                for wsb, dst in ((W0, qfm), (W1, kfm)):
                    for pt in range(NKT):
                        bi = pt % 2
                        pq = (PA, PB)[bi]
                        last = None
                        for kt in range(NKT):
                            w = (btok.get(bi),) if kt == 0 else ()
                            last = t_op(lambda h, kt=kt, pt=pt, wsb=wsb, pq=pq:
                                        h.matmul(
                                            pq[:],
                                            wsb[:, kt * C + pt * 128:
                                                kt * C + pt * 128 + 128],
                                            xh(kt), start=(kt == 0),
                                            stop=(kt == NKT - 1)), waits=w)
                        btok[bi] = elu_evac(
                            pq[:], dst[:, pt * SC:(pt + 1) * SC], SC, last)
                last = None
                for kt in range(NKT):
                    last = t_op(lambda h, kt=kt: h.matmul(
                        PSC[0:H, 0:SC], aWa_sb[:, kt * H:(kt + 1) * H], xh(kt),
                        start=(kt == 0), stop=(kt == NKT - 1)))
                tok_asc = s_op(lambda h: h.activation(
                    a_sc[:], PSC[0:H, 0:SC], AF.Sigmoid, bias=ba2[:, 0:1],
                    scale=1.0), waits=(last,))
                tok_vcp = tok_asc
                for ct in range(NCHS):
                    for half in range(2):
                        pk = (PA, PB)[half]
                        lastk = None
                        for kt in range(NKT):
                            w = (btok.get(half),) if kt == 0 else ()
                            lastk = t_op(lambda h, kt=kt, ct=ct, half=half, pk=pk:
                                         h.matmul(
                                             pk[:], xh(kt, ct * CH, CH),
                                             W1[:, kt * C + half * 512:
                                                 kt * C + half * 512 + 512],
                                             start=(kt == 0),
                                             stop=(kt == NKT - 1)), waits=w)
                        btok[half] = elu_evac(
                            pk[:], ktok[:, ct * C + half * 512:
                                        ct * C + half * 512 + 512], 512, lastk)
                        lastv = None
                        for kt in range(NKT):
                            w = (tok_vcp,) if kt == 0 else ()
                            lastv = t_op(lambda h, kt=kt, ct=ct, half=half:
                                         h.matmul(
                                             PSC[:], xh(kt, ct * CH, CH),
                                             W2[:, kt * C + half * 512:
                                                 kt * C + half * 512 + 512],
                                             start=(kt == 0),
                                             stop=(kt == NKT - 1)), waits=w)
                        tok_vcp = v_op(lambda h, ct=ct, half=half: h.tensor_copy(
                            va4[:, ct, half * 8:(half + 1) * 8, 0:64], PSC[:]),
                            waits=(lastv,))
                prog.barrier()
                if debug_stage == 7 and sc == 0:
                    dma_store(lambda h: h.dma_start(
                        dram_slice(out, 0), xres[:, 0:NKT * SC]))
                    prog.barrier()
                    do_flush()
                    return nc

                for ct in range(NCHS):
                    tok_c = v_op(lambda h, ct=ct: h.tensor_copy(
                        acp[:, 0:CH], a_sc[:, ct * CH:(ct + 1) * CH]))
                    cur, alt = 0, 2 * CH
                    for k in (1, 2, 4, 8, 16, 32, 64):
                        v_op(lambda h, k=k, cur=cur, alt=alt: h.tensor_copy(
                            acp[:, alt:alt + k], acp[:, cur:cur + k]))
                        tok_c = v_op(lambda h, k=k, cur=cur, alt=alt: h.tensor_mul(
                            acp[:, alt + k:alt + CH], acp[:, cur + k:cur + CH],
                            acp[:, cur:cur + CH - k]))
                        cur, alt = alt, cur
                    if cur != 0:
                        tok_c = v_op(lambda h, cur=cur: h.tensor_copy(
                            acp[:, 0:CH], acp[:, cur:cur + CH]))
                    tok_ai = v_op(lambda h: h.reciprocal(acp[:, CH:2 * CH],
                                                         acp[:, 0:CH]),
                                  waits=(tok_c,))
                    w = [tok_ai]
                    if tok_snew is not None:
                        w.append(tok_snew)
                    tok_ta = t_op(lambda h: h.transpose(
                        PSC[:, 0:H], acp[:, 0:CH], ident_f[0:H, 0:H]),
                        waits=tuple(w))
                    tok_tb = t_op(lambda h: h.transpose(
                        PSC[:, H:2 * H], acp[:, CH:2 * CH], ident_f[0:H, 0:H]))
                    tok_tc = t_op(lambda h: h.transpose(
                        PSC[0:1, 2 * H:3 * H], acp[:, CH - 1:CH],
                        ident_f[0:H, 0:H]))
                    tok_at = v_op(lambda h: h.tensor_copy(atok[:, 0:2 * H],
                                                          PSC[:, 0:2 * H]),
                                  waits=(tok_tc,))
                    tok_acr = v_op(lambda h: h.tensor_copy(
                        acr[:], PSC[0:1, 2 * H:3 * H]), waits=(tok_tc,))
                    tok_acm = t_op(lambda h: h.matmul(
                        PSC[0:64, 3 * H:4 * H], ones1_64f[:], acr[:],
                        start=True, stop=True), waits=(tok_acr, tok_at))
                    tok_acm2 = t_op(lambda h: h.matmul(
                        PSC[64:128, 3 * H:4 * H], ones1_64f[:], acr[:],
                        start=True, stop=True, tile_position=(0, 64)))
                    tok_acb = v_op(lambda h: h.tensor_copy(
                        acb2[:], PSC[:, 3 * H:4 * H]), waits=(tok_acm, tok_acm2))
                    if debug_stage == 61 and sc == 0 and ct == 0:
                        prog.barrier()
                        dma_store(lambda h: h.dma_start(
                            dram_slice(out, 0), xres[:, 0:NKT * SC]))
                        prog.barrier()
                        do_flush()
                        return nc
                    tok_vt = None
                    for hh in range(H):
                        tok_vt = v_op(lambda h, hh=hh, ct=ct: h.tensor_scalar(
                            vtil[:, hh * 65:(hh + 1) * 65],
                            vaug[:, ct * H * 65 + hh * 65:
                                 ct * H * 65 + (hh + 1) * 65],
                            atok[:, H + hh:H + hh + 1], None, op0=ALU.mult),
                            waits=(tok_at,) if hh == 0 else ())
                    if debug_stage == 621 and sc == 0 and ct == 0:
                        prog.barrier()
                        dma_store(lambda h: h.dma_start(
                            dram_slice(out, 0), xres[:, 0:NKT * SC]))
                        prog.barrier()
                        do_flush()
                        return nc
                    tok_pm = {}
                    for rnd in range(2):
                        toks_pr = {}
                        for j in range(4):
                            for par in range(2):
                                hh = 8 * rnd + 2 * j + par
                                off = par * 64
                                pg = (PA, PB)[par]
                                w = []
                                if j == 0 and tok_pm_used[par] is not None:
                                    w.append(tok_pm_used[par])
                                toks_pr[hh] = t_op(
                                    lambda h, hh=hh, ct=ct, off=off, j=j, pg=pg:
                                    h.matmul(
                                        pg[:, j * CH:(j + 1) * CH],
                                        kfm[off:off + 64,
                                            (hh // 2) * SC + ct * CH:
                                            (hh // 2) * SC + ct * CH + CH],
                                        qfm[off:off + 64,
                                            (hh // 2) * SC + ct * CH:
                                            (hh // 2) * SC + ct * CH + CH],
                                        start=True, stop=True,
                                        tile_position=(off, 0)),
                                    waits=tuple(w))
                        if debug_stage == 627 and sc == 0 and ct == 0 and rnd == 1:
                            prog.barrier()
                            dma_store(lambda h: h.dma_start(
                                dram_slice(out, 0), xres[:, 0:NKT * SC]))
                            prog.barrier()
                            do_flush()
                            return nc
                        if debug_stage in (623, 624) and sc == 0 and ct == 0:
                            prog.barrier()
                            dma_store(lambda h: h.dma_start(
                                dram_slice(out, 0), xres[:, 0:NKT * SC]))
                            prog.barrier()
                            do_flush()
                            return nc
                        for j in range(4):
                            for par in range(2):
                                hh = 8 * rnd + 2 * j + par
                                pg = (PA, PB)[par]
                                tok_pm[hh] = v_op(
                                    lambda h, hh=hh, j=j, pg=pg:
                                    h.scalar_tensor_tensor(
                                        pmb[:, hh * CH: hh * CH + CH],
                                        pg[:, j * CH:(j + 1) * CH],
                                        atok[:, H + hh:H + hh + 1],
                                        mask_st[:], ALU.mult, ALU.mult),
                                    waits=(toks_pr[8 * rnd + 6 + par], tok_at,
                                           pm_slot_tok.get(hh)))
                        tok_pm_used[0] = tok_pm[8 * rnd + 6]
                        tok_pm_used[1] = tok_pm[8 * rnd + 7]
                        if debug_stage == 625 and sc == 0 and ct == 0 and rnd == 0:
                            prog.barrier()
                            dma_store(lambda h: h.dma_start(
                                dram_slice(out, 0), xres[:, 0:NKT * SC]))
                            prog.barrier()
                            do_flush()
                            return nc
                    if debug_stage in (62, 628, 629) and sc == 0 and ct == 0:
                        prog.barrier()
                        dma_store(lambda h: h.dma_start(
                            dram_slice(out, 0), xres[:, 0:NKT * SC]))
                        prog.barrier()
                        do_flush()
                        return nc
                    tok_int = None
                    tok_u = None
                    for hh in range(H):
                        g = hh // 4
                        pslot = hh
                        off = (hh % 2) * 64
                        w = [tok_pm[hh]]
                        if hh == 0 and tok_outevac is not None:
                            w.append(tok_outevac)
                        pm_slot_tok[pslot] = t_op(
                            lambda h, hh=hh, ct=ct, pslot=pslot: h.matmul(
                                PNUM[:, NUMOFF[hh]:NUMOFF[hh] + 65],
                                pmb[:, pslot * CH:(pslot + 1) * CH],
                                vaug[:, ct * H * 65 + hh * 65:
                                     ct * H * 65 + (hh + 1) * 65],
                                start=True, stop=False), waits=tuple(w))
                        tok_int = t_op(lambda h, hh=hh, ct=ct, off=off: h.matmul(
                            PNUM[:, NUMOFF[hh]:NUMOFF[hh] + 65],
                            qfm[off:off + 64,
                                (hh // 2) * SC + ct * CH:
                                (hh // 2) * SC + ct * CH + CH],
                            saug[off:off + 64,
                                 UOFF[hh // 2]:UOFF[hh // 2] + 65],
                            start=False, stop=True))
                        wu = []
                        if hh == 0:
                            wu.append(tok_vt)
                            if tok_snew is not None:
                                wu.append(tok_snew)
                        tok_u = t_op(lambda h, hh=hh, ct=ct, off=off: h.matmul(
                            PU[off:off + 64,
                               UOFF[hh // 2]:UOFF[hh // 2] + 65],
                            ktok[:, ct * C + hh * 64: ct * C + hh * 64 + 64],
                            vtil[:, hh * 65:(hh + 1) * 65],
                            start=True, stop=True, tile_position=(0, off)),
                            waits=tuple(wu))
                    if debug_stage == 63 and sc == 0 and ct == 0:
                        prog.barrier()
                        dma_store(lambda h: h.dma_start(
                            dram_slice(out, 0), xres[:, 0:NKT * SC]))
                        prog.barrier()
                        do_flush()
                        return nc
                    dv0 = PNUM[:, 0:455].rearrange("p (h e) -> p h e", h=7, e=65)
                    dv1 = PNUM[:, 512:967].rearrange("p (h e) -> p h e", h=7, e=65)
                    dv2 = PNUM[:, 1024:1154].rearrange("p (h e) -> p h e", h=2, e=65)
                    tok_m = v_op(lambda h: h.tensor_mul(
                        dsml[:, 0:7], atok[:, 0:7], dv0[:, :, 64:65]),
                        waits=(tok_int,))
                    v_op(lambda h: h.tensor_mul(
                        dsml[:, 7:14], atok[:, 7:14], dv1[:, :, 64:65]))
                    v_op(lambda h: h.tensor_mul(
                        dsml[:, 14:16], atok[:, 14:16], dv2[:, :, 64:65]))
                    v_op(lambda h: h.tensor_scalar_add(dsml[:, 0:H],
                                                       dsml[:, 0:H], EPS_GLA))
                    v_op(lambda h: h.reciprocal(dsml[:, H:2 * H], dsml[:, 0:H]))
                    tok_m = v_op(lambda h: h.tensor_mul(
                        dsml[:, 0:H], dsml[:, H:2 * H], atok[:, 0:H]))
                    for hh in range(H):
                        tok_outevac = v_op(lambda h, hh=hh, ct=ct: h.tensor_scalar(
                            otok[:, ct * C + hh * 64: ct * C + hh * 64 + 64],
                            PNUM[:, NUMOFF[hh]: NUMOFF[hh] + 64],
                            dsml[:, hh:hh + 1], None, op0=ALU.mult),
                            waits=(tok_m,) if hh == 0 else ())
                    tok_sa = v_op(lambda h: h.tensor_add(sscr[:], saug[:], PU[:]),
                                  waits=(tok_u, tok_int))
                    for hh in range(H):
                        off = (hh % 2) * 64
                        tok_snew = v_op(lambda h, hh=hh, off=off: h.tensor_scalar(
                            saug[off:off + 64,
                                 UOFF[hh // 2]:UOFF[hh // 2] + 65],
                            sscr[off:off + 64,
                                 UOFF[hh // 2]:UOFF[hh // 2] + 65],
                            acb2[off:off + 64, hh:hh + 1], None, op0=ALU.mult),
                            waits=(tok_acb, tok_sa) if hh == 0 else ())
                prog.barrier()
                if debug_stage == 6 and sc == 0:
                    dma_store(lambda h: h.dma_start(
                        dram_slice(out, 0), xres[:, 0:NKT * SC]))
                    prog.barrier()
                    do_flush()
                    return nc

                ttok = {0: None, 1: None}
                for ct in range(NCHS):
                    for grp in range(2):
                        ptr = (PNUM, PU)[grp]
                        tts = []
                        for j in range(4):
                            jj = grp * 4 + j
                            w = (ttok[grp],) if j == 0 else ()
                            tts.append(t_op(lambda h, ct=ct, jj=jj, j=j, ptr=ptr:
                                            h.transpose(
                                                ptr[:, j * 128:(j + 1) * 128],
                                                otok[:, ct * C + jj * 128:
                                                     ct * C + jj * 128 + 128],
                                                ident_f[:]), waits=w))
                        for j in range(4):
                            jj = grp * 4 + j
                            ttok[grp] = v_op(lambda h, ct=ct, jj=jj, j=j, ptr=ptr:
                                             h.tensor_copy(
                                                 ofm[:, jj * SC + ct * CH:
                                                     jj * SC + ct * CH + CH],
                                                 ptr[:, j * 128:(j + 1) * 128]),
                                             waits=(tts[3],))
                prog.barrier()
                wtok = {}
                for dmt in range(NKT):
                    bi = dmt % 2
                    pq = (PA, PB)[bi]
                    last = None
                    for pt in range(NKT):
                        w = (wtok.get(bi),) if pt == 0 else ()
                        last = t_op(lambda h, pt=pt, dmt=dmt, pq=pq: h.matmul(
                            pq[:],
                            W3[:, pt * C + dmt * 128: pt * C + dmt * 128 + 128],
                            ofm[:, pt * SC:(pt + 1) * SC],
                            start=(pt == 0), stop=(pt == NKT - 1)), waits=w)
                    wtok[bi] = v_op(lambda h, dmt=dmt, pq=pq: h.tensor_add(
                        xres[:, xo + dmt * SC: xo + (dmt + 1) * SC],
                        xres[:, xo + dmt * SC: xo + (dmt + 1) * SC], pq[:]),
                        waits=(last, tok_xr))
                toks = prog.force_tokens()
                dst_dram = out if debug_stage == 1 else xs1
                dma_store(lambda h, sc=sc, dst_dram=dst_dram: h.dma_start(
                    dram_slice(dst_dram, sc), xres[:, xo:xo + NKT * SC]),
                    waits=toks)
                prog.barrier()
            do_flush()

        if debug_stage == 1:
            prog.barrier()
            do_flush()
            return nc

        _ffn_n = [0]

        def ffn_phase(src_dram, dst_dram, reload_weights):
            _ffn_n[0] += 1
            sfx = str(_ffn_n[0])
            with ExitStack() as ph:
                def psb(name, shape, dt=BF16):
                    return ph.enter_context(nc.sbuf_tensor(name + sfx, shape, dt))

                def pps(name, shape, dt=F32):
                    return ph.enter_context(nc.psum_tensor(name + sfx, shape, dt))

                xhf = psb("f_xh", [128, NKT * (SC + 1)])
                halo = psb("f_halo", [128, NKT])
                dbuf = psb("f_d", [128, NKT * SC])
                xrb = psb("f_xr", [128, NKT * SC])
                xkb = psb("f_xk", [128, NKT * SC])
                rb = psb("f_r", [128, NKT * SC])
                k2b = psb("f_k2", [128, NKT * SC])
                FP = [pps(f"f_P{i}", [128, 512]) for i in range(4)]

                if reload_weights:
                    wload(W0, "fWr"); wload(W1, "fWk"); wload(W2, "fWv")
                xhv = xhf[:].rearrange("p (k s) -> p k s", k=NKT, s=SC + 1)
                for sc in range(NSC):
                    tok_xh, tok_xr = ln_slice(sc, src_dram, xhf, 0, SC + 1, 0,
                                              FP[0], FP[1], halo=1)
                    if sc == 0:
                        v_op(lambda h: h.memset(xhv[:, :, 0:1], 0.0))
                    else:
                        v_op(lambda h: h.tensor_copy(xhv[:, :, 0:1],
                                                     halo[:].rearrange(
                                                         "p (k s) -> p k s",
                                                         k=NKT, s=1)))
                    prog.barrier()
                    v_op(lambda h: h.tensor_copy(
                        halo[:].rearrange("p (k s) -> p k s", k=NKT, s=1),
                        xhv[:, :, SC:SC + 1]))
                    tok_d = v_op(lambda h: h.tensor_sub(
                        dbuf[:].rearrange("p (k s) -> p k s", k=NKT, s=SC),
                        xhv[:, :, 1:SC + 1], xhv[:, :, 0:SC]))
                    for kt in range(NKT):
                        v_op(lambda h, kt=kt: h.scalar_tensor_tensor(
                            xrb[:, kt * SC:(kt + 1) * SC],
                            dbuf[:, kt * SC:(kt + 1) * SC],
                            vec_sb[:, 24 + kt:25 + kt],
                            xhf[:, kt * (SC + 1): kt * (SC + 1) + SC],
                            ALU.mult, ALU.add))
                        v_op(lambda h, kt=kt: h.scalar_tensor_tensor(
                            xkb[:, kt * SC:(kt + 1) * SC],
                            dbuf[:, kt * SC:(kt + 1) * SC],
                            vec_sb[:, 32 + kt:33 + kt],
                            xhf[:, kt * (SC + 1): kt * (SC + 1) + SC],
                            ALU.mult, ALU.add))
                    prog.barrier()
                    btok = {}
                    for which, (wsb, xin_b) in enumerate(((W0, xrb), (W1, xkb))):
                        for pt in range(NKT):
                            bi = (which * NKT + pt) % 4
                            pq = FP[bi]
                            last = None
                            for kt in range(NKT):
                                w = (btok.get(bi),) if kt == 0 else ()
                                last = t_op(
                                    lambda h, kt=kt, pt=pt, wsb=wsb, xin_b=xin_b,
                                    pq=pq: h.matmul(
                                        pq[:],
                                        wsb[:, kt * C + pt * 128:
                                            kt * C + pt * 128 + 128],
                                        xin_b[:, kt * SC:(kt + 1) * SC],
                                        start=(kt == 0), stop=(kt == NKT - 1)),
                                    waits=w)
                            if which == 0:
                                btok[bi] = s_op(
                                    lambda h, pt=pt, pq=pq: h.activation(
                                        rb[:, pt * SC:(pt + 1) * SC], pq[:],
                                        AF.Sigmoid, bias=0.0, scale=1.0),
                                    waits=(last,))
                            else:
                                t1 = s_op(lambda h, pq=pq: h.activation(
                                    escr1[:, 0:SC], pq[:], AF.Relu,
                                    bias=0.0, scale=1.0), waits=(last,))
                                btok[bi] = s_op(
                                    lambda h, pt=pt: h.activation(
                                        k2b[:, pt * SC:(pt + 1) * SC],
                                        escr1[:, 0:SC], AF.Square,
                                        bias=0.0, scale=1.0), waits=(t1,))
                    prog.barrier()
                    for pt in range(NKT):
                        bi = pt % 4
                        pq = FP[bi]
                        last = None
                        for kt in range(NKT):
                            w = (btok.get(bi),) if kt == 0 else ()
                            last = t_op(lambda h, kt=kt, pt=pt, pq=pq: h.matmul(
                                pq[:],
                                W2[:, kt * C + pt * 128: kt * C + pt * 128 + 128],
                                k2b[:, kt * SC:(kt + 1) * SC],
                                start=(kt == 0), stop=(kt == NKT - 1)), waits=w)
                        t1 = v_op(lambda h, pt=pt, pq=pq: h.tensor_mul(
                            scrA[:], pq[:], rb[:, pt * SC:(pt + 1) * SC]),
                            waits=(last,))
                        btok[bi] = v_op(lambda h, pt=pt: h.tensor_add(
                            xres[:, pt * SC:(pt + 1) * SC],
                            xres[:, pt * SC:(pt + 1) * SC], scrA[:]),
                            waits=(tok_xr,))
                    toks = prog.force_tokens()
                    dma_store(lambda h, sc=sc, dst_dram=dst_dram: h.dma_start(
                        dram_slice(dst_dram, sc), xres[:, 0:NKT * SC]),
                        waits=toks)
                    prog.barrier()
                do_flush()

        ffn_phase(xs1, out if debug_stage == 2 else xs2, True)
        if debug_stage == 2:
            prog.barrier()
            do_flush()
            return nc

        vaugd = nc.dram_tensor("vaugd", [128, 16 * H * 65], BF16)
        aoBd = nc.dram_tensor("aoBd", [128, NKT * T], BF16)
        with ExitStack() as ph:
            def psb(name, shape, dt=BF16):
                return ph.enter_context(nc.sbuf_tensor(name, shape, dt))

            def pps(name, shape, dt=F32):
                return ph.enter_context(nc.psum_tensor(name, shape, dt))

            xh3 = psb("b_xh3", [128, NKT * T])
            qup = psb("b_qu", [128, T])
            qvp = psb("b_qv", [128, T])
            kp = psb("b_k", [128, T])
            abp = psb("b_ab", [128, T + 8])
            mab = psb("b_mab", [128, T])
            mbb = psb("b_mbb", [128, T])
            ptb = psb("b_pt", [128, 2 * 1024])
            vtmp = psb("b_vtmp", [128, 2 * H * 65])
            vhb = psb("b_vh", [128, 2 * 16 * 65])
            densb = psb("b_den", [1, 512], F32)
            denbc = psb("b_denbc", [64, 512], F32)
            dinv = psb("b_dinv", [64, 512], F32)
            drec = psb("b_drec", [64, 512], F32)
            otmp = psb("b_otmp", [64, 1024])
            cs1 = psb("b_cs1", [128, 128], F32)
            cs2 = psb("b_cs2", [128, 128], F32)
            d2s = psb("b_d2s", [128, 128], F32)
            d1s = psb("b_d1s", [128, 128], F32)
            c1s = psb("b_c1s", [128, 128], F32)
            c2s = psb("b_c2s", [128, 128], F32)

            SQ = [pps(f"b_SQ{i}", [128, 512]) for i in range(4)]
            AV = [pps(f"b_AV{i}", [128, 1024]) for i in range(2)]

            wload(W0, "bWq"); wload(W1, "bWk"); wload(W2, "bWv"); wload(W3, "bWo")

            for sc in range(NSC):
                ln_slice(sc, xs2, xh3, sc * SC, T, 0, SQ[0], SQ[1],
                         load_res=False)
                prog.barrier()

            if debug_stage == 31:
                prog.barrier()
                dma_store(lambda h: h.dma_start(
                    dram_slice(out, 0), xres[:, 0:NKT * SC]))
                prog.barrier()
                do_flush()
                return nc

            vt4 = vtmp[:].rearrange("p (s h e) -> p s h e", s=2, h=H, e=65)
            v_op(lambda h: h.memset(vt4[:, :, :, 64:65], 1.0))
            prog.barrier()
            vtok = {}
            for tc in range(16):
                sl = tc % 2
                pv = AV[sl]
                last = None
                for half in range(2):
                    for kt in range(NKT):
                        w = (vtok.get(sl),) if (kt == 0 and half == 0) else ()
                        last = t_op(lambda h, kt=kt, tc=tc, pv=pv, half=half:
                                    h.matmul(
                            pv[:, half * 512:(half + 1) * 512],
                            xh3[:, kt * T + tc * 128: kt * T + tc * 128 + 128],
                            W2[:, kt * C + half * 512:
                                kt * C + half * 512 + 512],
                            start=(kt == 0), stop=(kt == NKT - 1)), waits=w)
                tev = v_op(lambda h, sl=sl, pv=pv: h.tensor_copy(
                    vt4[:, sl, :, 0:64], pv[:]), waits=(last,))
                vtok[sl] = dma_store(lambda h, tc=tc, sl=sl: h.dma_start(
                    bass.AP(vaugd, tc * H * 65,
                            [[16 * H * 65, 128], [1, H * 65]]),
                    vtmp[:, sl * H * 65:(sl + 1) * H * 65]), waits=(tev,))
            prog.barrier()

            if debug_stage == 32:
                prog.barrier()
                dma_store(lambda h: h.dma_start(
                    dram_slice(out, 0), xres[:, 0:NKT * SC]))
                prog.barrier()
                do_flush()
                return nc

            tok_vh = {}
            for pr in range(8):
                for par in range(2):
                    hh = 2 * pr + par
                    for jt in range(16):
                        tok_vh[par] = dma_sync(
                            lambda h, hh=hh, par=par, jt=jt: h.dma_start(
                                vhb[:, par * 1040 + jt * 65:
                                    par * 1040 + jt * 65 + 65],
                                bass.AP(vaugd, jt * H * 65 + hh * 65,
                                        [[16 * H * 65, 128], [1, 65]])))
                btok = {}
                for which, wsb in ((0, W0), (1, W1)):
                    for qc in range(4):
                        bi = (which * 4 + qc) % 4
                        pq = SQ[bi]
                        last = None
                        for kt in range(NKT):
                            w = (btok.get(bi),) if kt == 0 else ()
                            last = t_op(lambda h, kt=kt, qc=qc, wsb=wsb, pq=pq,
                                        pr=pr: h.matmul(
                                pq[:],
                                wsb[:, kt * C + pr * 128: kt * C + pr * 128 + 128],
                                xh3[:, kt * T + qc * 512: kt * T + qc * 512 + 512],
                                start=(kt == 0), stop=(kt == NKT - 1)), waits=w)
                        if which == 0:
                            v_op(lambda h, qc=qc, pq=pq, pr=pr: h.tensor_scalar(
                                qup[:, qc * 512:(qc + 1) * 512], pq[:],
                                vec_sb[:, pr:pr + 1], None, op0=ALU.add),
                                waits=(last,))
                            btok[bi] = v_op(
                                lambda h, qc=qc, pq=pq, pr=pr: h.tensor_scalar(
                                    qvp[:, qc * 512:(qc + 1) * 512], pq[:],
                                    vec_sb[:, 8 + pr:9 + pr], None,
                                    op0=ALU.add))
                        else:
                            btok[bi] = v_op(lambda h, qc=qc, pq=pq: h.tensor_copy(
                                kp[:, qc * 512:(qc + 1) * 512], pq[:]),
                                waits=(last,))
                prog.barrier()
                if debug_stage == 33 and pr == 0:
                    dma_store(lambda h: h.dma_start(
                        dram_slice(out, 0), xres[:, 0:NKT * SC]))
                    prog.barrier()
                    do_flush()
                    return nc
                v_op(lambda h: h.tensor_mul(mab[:], qvp[:], tswap[:]))
                tok_mb = v_op(lambda h: h.tensor_mul(mbb[:], qvp[:], t1dup[:]))
                v_op(lambda h: h.memset(abp[:, T:T + 8], 0.0))
                for qc in range(4):
                    pq = SQ[qc]
                    mms = []
                    for par in range(2):
                        ro = par * 64
                        mms.append(t_op(lambda h, qc=qc, ro=ro, pq=pq: h.matmul(
                            pq[0 + ro:32 + ro, :],
                            fgm[ro:ro + 64, 0:32],
                            mab[ro:ro + 64, qc * 512:(qc + 1) * 512],
                            start=True, stop=True, tile_position=(ro, ro)),
                            waits=(tok_mb,) if qc == 0 and par == 0 else ()))
                        mms.append(t_op(lambda h, qc=qc, ro=ro, pq=pq: h.matmul(
                            pq[32 + ro:64 + ro, :],
                            fgm[ro:ro + 64, 32:64],
                            mbb[ro:ro + 64, qc * 512:(qc + 1) * 512],
                            start=True, stop=True,
                            tile_position=(ro, ro + 32))))
                    v_op(lambda h, qc=qc, pq=pq: h.tensor_copy(
                        abp[:, qc * 512:(qc + 1) * 512], pq[:]), waits=(mms[-1],))
                prog.barrier()
                if debug_stage == 34 and pr == 0:
                    dma_store(lambda h: h.dma_start(
                        dram_slice(out, 0), xres[:, 0:NKT * SC]))
                    prog.barrier()
                    do_flush()
                    return nc

                for par in range(2):
                    hh = 2 * pr + par
                    av = AV[par]
                    ro = par * 64
                    for qh in range(2):
                        sqA, sqB = (SQ[0], SQ[1]) if par == 0 else (SQ[2], SQ[3])
                        exp_tok = {0: None, 1: None}
                        slot_lastuse = {0: None, 1: None}
                        freeslot_tok = None
                        attnv_tok = None
                        for jt in range(16):
                            ptslot = jt % 2
                            jq = jt * 128
                            dq = -1
                            for qi in (0, 1):
                                q0 = qh * 1024 + qi * 512
                                if q0 <= jq < q0 + 512:
                                    dq = qi
                            qlist = [1, 0] if dq == 0 else [0, 1]
                            pt_done = None
                            for qi in qlist:
                                sq = (sqA, sqB)[qi]
                                q0 = qh * 1024 + qi * 512
                                w0 = [exp_tok[qi], slot_lastuse[qi]]
                                t_op(lambda h, jt=jt, q0=q0, ro=ro, sq=sq:
                                     h.matmul(
                                         sq[:],
                                         kp[ro:ro + 64, jt * 128: jt * 128 + 128],
                                         qup[ro:ro + 64, q0:q0 + 512],
                                         start=True, stop=False,
                                         tile_position=(ro, 0)),
                                     waits=tuple(t for t in w0 if t))
                                zlast = None
                                for ti in range(4):
                                    qt0 = q0 + ti * 128
                                    Qhat = qt0 // 128
                                    stop = (ti == 3)
                                    if Qhat >= jt:
                                        zlast = t_op(
                                            lambda h, jt=jt, qt0=qt0, ro=ro,
                                            sq=sq, ti=ti, stop=stop: h.matmul(
                                                sq[:, ti * 128:(ti + 1) * 128],
                                                t1dup[ro:ro + 64,
                                                      jt * 128: jt * 128 + 128],
                                                abp[ro:ro + 64, qt0:qt0 + 128],
                                                start=False, stop=stop,
                                                tile_position=(ro, 0)))
                                    else:
                                        zlast = t_op(
                                            lambda h, jt=jt, qt0=qt0, ro=ro,
                                            sq=sq, ti=ti, stop=stop: h.matmul(
                                                sq[:, ti * 128:(ti + 1) * 128],
                                                t2dup[ro:ro + 64,
                                                      jt * 128: jt * 128 + 128],
                                                abp[ro:ro + 64,
                                                    qt0 + 1:qt0 + 129],
                                                start=False, stop=stop,
                                                tile_position=(ro, 0)))
                                corr_tok = None
                                if dq == qi and debug_stage != 351 \
                                        and not (debug_stage == 3523):
                                    sqo = (sqA, sqB)[1 - qi]
                                    ti = (jq - q0) // 128
                                    qt0 = q0 + ti * 128
                                    fs_w = [exp_tok[1 - qi], freeslot_tok,
                                            slot_lastuse[1 - qi]]
                                    md2 = t_op(lambda h, jt=jt, qt0=qt0, ro=ro,
                                               sqo=sqo: h.matmul(
                                        sqo[:, 0:128],
                                        t2dup[ro:ro + 64,
                                              jt * 128:jt * 128 + 128],
                                        abp[ro:ro + 64, qt0 + 1:qt0 + 129],
                                        start=True, stop=True,
                                        tile_position=(ro, 0)),
                                        waits=tuple(t for t in fs_w if t))
                                    md1 = t_op(lambda h, jt=jt, qt0=qt0, ro=ro,
                                               sqo=sqo: h.matmul(
                                        sqo[:, 128:256],
                                        t1dup[ro:ro + 64,
                                              jt * 128:jt * 128 + 128],
                                        abp[ro:ro + 64, qt0:qt0 + 128],
                                        start=True, stop=True,
                                        tile_position=(ro, 0)))
                                    s1 = s_op(lambda h, sqo=sqo: h.copy(
                                        d2s[:], sqo[:, 0:128]), waits=(md1,))
                                    s2 = s_op(lambda h, sqo=sqo: h.copy(
                                        d1s[:], sqo[:, 128:256]))
                                    if debug_stage == 3521:
                                        freeslot_tok = s2
                                        slot_lastuse[1 - qi] = s2
                                        corr_tok = s2
                                    else:
                                        g1 = g_op(lambda h: h.tensor_mul(
                                            c1s[:], d2s[:], m2diag[:]),
                                            waits=(s1,))
                                        g2 = g_op(lambda h: h.tensor_mul(
                                            c2s[:], d1s[:], m2up1[:]),
                                            waits=(s2,))
                                        g3 = g_op(lambda h: h.tensor_sub(
                                            c1s[:], c1s[:], c2s[:]), waits=(g2,))
                                        corr_tok = v_op(
                                            lambda h, ti=ti, sq=sq: h.tensor_add(
                                                sq[:, ti * 128:(ti + 1) * 128],
                                                sq[:, ti * 128:(ti + 1) * 128],
                                                c1s[:]), waits=(g3, zlast))
                                        freeslot_tok = s2
                                        slot_lastuse[1 - qi] = s2
                                cellq = jt * 128 - 1
                                if jt > 0 and q0 <= cellq < q0 + 512 \
                                        and debug_stage not in (351, 3521, 3522):
                                    sqo = (sqA, sqB)[1 - qi]
                                    qc0 = (jt - 1) * 128
                                    mcell = t_op(lambda h, jt=jt, ro=ro,
                                                 qc0=qc0, sqo=sqo: h.matmul(
                                        sqo[:, 256:384],
                                        t2dup[ro:ro + 64,
                                              jt * 128:jt * 128 + 128],
                                        abp[ro:ro + 64, qc0 + 1:qc0 + 129],
                                        start=True, stop=True,
                                        tile_position=(ro, 0)),
                                        waits=tuple(
                                            t for t in (exp_tok[1 - qi],
                                                        freeslot_tok,
                                                        slot_lastuse[1 - qi])
                                            if t))
                                    sc1 = s_op(lambda h, sqo=sqo: h.copy(
                                        cs1[:], sqo[:, 256:384]),
                                        waits=(mcell,))
                                    gc = g_op(lambda h: h.tensor_mul(
                                        cs2[:], cs1[:], mcmask[:]),
                                        waits=(sc1,))
                                    tic = (qc0 - q0) // 128
                                    corr_tok = v_op(
                                        lambda h, tic=tic, sq=sq:
                                        h.tensor_sub(
                                            sq[:, tic * 128:(tic + 1) * 128],
                                            sq[:, tic * 128:(tic + 1) * 128],
                                            cs2[:]),
                                        waits=(gc, zlast, corr_tok))
                                    freeslot_tok = sc1
                                    slot_lastuse[1 - qi] = sc1
                                wexp = [zlast, corr_tok]
                                exp_tok[qi] = s_op(
                                    lambda h, qi=qi, ptslot=ptslot, sq=sq:
                                    h.activation(
                                        ptb[:, ptslot * 1024 + qi * 512:
                                            ptslot * 1024 + qi * 512 + 512],
                                        sq[:], AF.Exp, bias=SMXB_AP,
                                        scale=0.125),
                                    waits=tuple(t for t in wexp if t))
                                pt_done = exp_tok[qi]
                            wv = [pt_done]
                            if jt == 0:
                                wv.append(tok_vh[par])
                            if debug_stage in (351, 352, 3521, 3522, 3523):
                                continue
                            for half in range(2):
                                attnv_tok = t_op(
                                    lambda h, jt=jt, par=par, ptslot=ptslot,
                                    av=av, half=half: h.matmul(
                                        av[0:65, half * 512:(half + 1) * 512],
                                        vhb[:, par * 1040 + jt * 65:
                                            par * 1040 + jt * 65 + 65],
                                        ptb[:, ptslot * 1024 + half * 512:
                                            ptslot * 1024 + half * 512 + 512],
                                        start=(jt == 0), stop=(jt == 15)),
                                    waits=tuple(t for t in wv if t))
                        if debug_stage in (351, 352, 353, 3521, 3522, 3523):
                            prog.barrier()
                            if pr == 0 and par == 0 and qh == 0:
                                dma_store(lambda h: h.dma_start(
                                    dram_slice(out, 0), xres[:, 0:NKT * SC]))
                                prog.barrier()
                                do_flush()
                                return nc
                            continue
                        for bi in range(2):
                            sq = (sqA, sqB)[bi]
                            t1 = v_op(lambda h, bi=bi, av=av: h.tensor_copy(
                                densb[:], av[64:65, bi * 512:(bi + 1) * 512]),
                                waits=(attnv_tok,))
                            bcm = t_op(lambda h, sq=sq: h.matmul(
                                sq[0:64, 0:512], ones1_64f[:], densb[:],
                                start=True, stop=True),
                                waits=(t1, exp_tok[bi], freeslot_tok,
                                       slot_lastuse[bi]))
                            v_op(lambda h, sq=sq: h.tensor_copy(
                                denbc[:], sq[0:64, 0:512]), waits=(bcm,))
                            td1 = s_op(lambda h: h.activation(
                                drec[:], denbc[:], AF.Ln, bias=0.0, scale=1.0),
                                waits=(prog.chain_tok["vector"],))
                            td2 = s_op(lambda h: h.activation(
                                dinv[:], drec[:], AF.Exp, bias=0.0, scale=-1.0),
                                waits=(td1,))
                            tok_o = v_op(lambda h, bi=bi, av=av: h.tensor_mul(
                                otmp[:, bi * 512:(bi + 1) * 512],
                                av[0:64, bi * 512:(bi + 1) * 512], dinv[:]),
                                waits=(td2,))
                        tok_o = dma_cast((lambda h, hh=hh, qh=qh,
                                          par=par: h.dma_start(
                            bass.AP(aoBd,
                                    par * 64 * NKT * T + (hh // 2) * T
                                    + qh * 1024,
                                    [[NKT * T, 64], [1, 1024]]),
                            otmp[:])), waits=(tok_o,))
                        prog.barrier()
                        if debug_stage == 35 and pr == 0 and par == 0 and qh == 0:
                            dma_store(lambda h: h.dma_start(
                                dram_slice(out, 0), xres[:, 0:NKT * SC]))
                            prog.barrier()
                            do_flush()
                            return nc
            for sc in range(NSC):
                tok_xr = dma_sync(lambda h, sc=sc: h.dma_start(
                    xres[:, 0:NKT * SC], dram_slice(xs2, sc)))
                tok_ai = dma_sync(lambda h, sc=sc: h.dma_start(
                    xh3[:, 0:NKT * SC],
                    bass.AP(aoBd, sc * SC,
                            [[NKT * T, 128], [T, NKT], [1, SC]])))
                btok = {}
                for dmt in range(NKT):
                    bi = dmt % 4
                    pq = SQ[bi]
                    last = None
                    for pt in range(NKT):
                        w = [btok.get(bi)] if pt == 0 else []
                        if pt == 0 and dmt == 0:
                            w.append(tok_ai)
                        last = t_op(lambda h, pt=pt, dmt=dmt, pq=pq, sc=sc:
                                    h.matmul(
                            pq[:],
                            W3[:, pt * C + dmt * 128: pt * C + dmt * 128 + 128],
                            xh3[:, pt * 512:(pt + 1) * 512],
                            start=(pt == 0), stop=(pt == NKT - 1)),
                            waits=tuple(t for t in w if t))
                    t1 = v_op(lambda h, dmt=dmt, pq=pq: h.tensor_scalar(
                        scrB[:], pq[:], vec_sb[:, 16 + dmt:17 + dmt], None,
                        op0=ALU.add), waits=(last,))
                    btok[bi] = v_op(lambda h, dmt=dmt: h.tensor_add(
                        xres[:, dmt * SC:(dmt + 1) * SC],
                        xres[:, dmt * SC:(dmt + 1) * SC], scrB[:]),
                        waits=(tok_xr,))
                toks = prog.force_tokens()
                dst3 = out if debug_stage == 3 else xs3
                dma_store(lambda h, sc=sc, dst3=dst3: h.dma_start(
                    dram_slice(dst3, sc), xres[:, 0:NKT * SC]), waits=toks)
                prog.barrier()
            do_flush()

        if debug_stage == 3:
            prog.barrier()
            do_flush()
            return nc

        ffn_phase(xs3, out, True)
        prog.barrier()
        do_flush()
        return nc


def _pack_fm(a2d):
    return np.ascontiguousarray(
        a2d.reshape(NKT, 128, T).transpose(1, 0, 2).reshape(128, NKT * T))


def _unpack_fm(a):
    return a.reshape(128, NKT, T).transpose(1, 0, 2).reshape(NKT * 128, T)


def _vec_pack(v):
    return np.ascontiguousarray(np.asarray(v, np.float32).reshape(8, 128).T)


def kernel(**inputs):
    x = np.asarray(inputs["L0_attn_a_x"], np.float32)
    w = {
        "aWq": inputs["L0_attn_a_Wq"], "aWk": inputs["L0_attn_a_Wk"],
        "aWv": inputs["L0_attn_a_Wv"], "aWo": inputs["L0_attn_a_Wo"],
        "aWa": inputs["L0_attn_a_Wa"],
        "bWq": inputs["L0_attn_b_Wq"], "bWk": inputs["L0_attn_b_Wk"],
        "bWv": inputs["L0_attn_b_Wv"], "bWo": inputs["L0_attn_b_Wo"],
        "fWr": np.asarray(inputs["L0_ffn_W_r"], np.float32).T,
        "fWk": np.asarray(inputs["L0_ffn_W_k"], np.float32).T,
        "fWv": np.asarray(inputs["L0_ffn_W_v"], np.float32).T,
    }
    w = {k: np.ascontiguousarray(np.asarray(v, np.float32)) for k, v in w.items()}
    w["b_u"] = _vec_pack(np.asarray(inputs["L0_attn_b_u"]).reshape(-1))
    w["b_v"] = _vec_pack(np.asarray(inputs["L0_attn_b_v"]).reshape(-1))
    w["b_bo"] = _vec_pack(inputs["L0_attn_b_bo"])
    w["f_mur"] = _vec_pack(inputs["L0_ffn_mu_r"])
    w["f_muk"] = _vec_pack(inputs["L0_ffn_mu_k"])
    w["a_ba"] = np.ascontiguousarray(
        np.asarray(inputs["L0_attn_a_ba"], np.float32).reshape(H, 1))

    nc = build()
    in_maps = []
    for b in range(8):
        m = dict(w)
        m["xin"] = _pack_fm(np.ascontiguousarray(x[b].T))
        in_maps.append(m)
    res = run_bass_kernel_spmd(nc, in_maps, core_ids=list(range(8)))
    outs = []
    for b in range(8):
        outs.append(_unpack_fm(np.asarray(res.results[b]["out"])).T)
    return np.stack(outs).astype(np.float32)
